# version 01 (final)
# baseline (speedup 1.0000x reference)
import numpy as np
from contextlib import ExitStack
import concourse.bass as bass
import concourse.mybir as mybir
from concourse.bass_utils import run_bass_kernel_spmd

F32 = mybir.dt.float32
BF16 = mybir.dt.bfloat16
AF = mybir.ActivationFunctionType
ALU = mybir.AluOpType
AX = mybir.AxisListType

ENGINES = ("tensor", "vector", "scalar", "gpsimd", "sync")
NCORES = 8


class Em:

    def __init__(self, nc):
        self.nc = nc
        self.ops = {e: [] for e in ENGINES}
        self.val = {e: 0 for e in ENGINES}
        self.last_w = {}
        self.readers = {}
        self.waited = {e: {} for e in ENGINES}

    def go(self, eng, fname, *args, rd=(), wr=(), inc=None, **kw):
        if inc is None:
            inc = 16 if fname == "dma_start" else 1
        waits = {}

        def need(p_eng, v):
            if p_eng == eng and eng == "tensor":
                return
            if self.waited[eng].get(p_eng, 0) >= v:
                return
            if waits.get(p_eng, 0) < v:
                waits[p_eng] = v

        rd = [id(x) for x in rd]
        wr = [id(x) for x in wr]
        for k in rd:
            if k in self.last_w:
                need(*self.last_w[k])
        for k in wr:
            if k in self.last_w:
                need(*self.last_w[k])
            for p_eng, v in self.readers.get(k, {}).items():
                need(p_eng, v)
        self.val[eng] += inc
        myval = self.val[eng]
        for k in wr:
            self.last_w[k] = (eng, myval)
            self.readers[k] = {}
        for k in rd:
            self.readers.setdefault(k, {})
            self.readers[k][eng] = myval
        for p_eng, v in waits.items():
            self.waited[eng][p_eng] = v
        wl = list(waits.items())
        self.ops[eng].append((wl, fname, args, kw, inc))
        return myval

    def barrier(self):
        snap = dict(self.val)
        for e in ENGINES:
            wl = [(p, v) for p, v in snap.items() if p != e and v > 0
                  and self.waited[e].get(p, 0) < v]
            for p, v in wl:
                self.waited[e][p] = v
            if wl:
                self.ops[e].append((wl, None, None, None, 0))

    def finalize(self, block, sems):
        for q in ("sync", "gpsimd"):
            if self.val[q]:
                self.ops[q].append(([(q, self.val[q])], None, None, None, 0))
        for ename in ENGINES:
            oplist = self.ops[ename]
            if not oplist:
                continue

            def make(ename, oplist):
                def body(eng):
                    for wl, fname, args, kw, inc in oplist:
                        for (p_eng, v) in wl:
                            eng.wait_ge(sems[p_eng], v)
                        if fname is None:
                            continue
                        inst = getattr(eng, fname)(*args, **kw)
                        inst.then_inc(sems[ename], inc)
                return body
            getattr(block, ename)(make(ename, oplist))


class B:

    def __init__(self, name):
        self.nc = bass.Bass("TRN2", target_bir_lowering=False)
        self.stack = ExitStack()
        self.em = Em(self.nc)
        self.n = 0
        self._cc = {}

    def ccol(self, v):
        if v not in self._cc:
            t = self.sb(128, 1)
            self.memset(t[:, 0:1], float(v), wr=[t])
            self._cc[v] = t
        return self._cc[v]

    def _nm(self, p):
        self.n += 1
        return "%s%d" % (p, self.n)

    def inp(self, name, shape, dt=F32):
        return self.nc.dram_tensor(name, list(shape), dt, kind="ExternalInput").ap()

    def outp(self, name, shape, dt=F32):
        return self.nc.dram_tensor(name, list(shape), dt, kind="ExternalOutput").ap()

    def sb(self, p, f, dt=F32):
        return self.stack.enter_context(self.nc.sbuf_tensor(self._nm("s"), [p, f], dt))

    def sbx(self, st, p, f, dt=F32):
        return st.enter_context(self.nc.sbuf_tensor(self._nm("s"), [p, f], dt))

    def ps(self, p, f):
        return self.stack.enter_context(self.nc.psum_tensor(self._nm("p"), [p, f], F32))

    def dma(self, dst, src, rd=(), wr=(), q="sync"):
        self.em.go(q, "dma_start", dst, src, rd=rd, wr=wr)

    def mm(self, out, lhsT, rhs, start, stop, rd=(), wr=()):
        self.em.go("tensor", "matmul", out, lhsT, rhs, start=start, stop=stop,
                   rd=rd, wr=wr)

    def transp(self, out, in_, ident, rd=(), wr=()):
        self.em.go("tensor", "transpose", out, in_, ident, rd=rd, wr=wr)

    def tt(self, op, out, a, b, rd=(), wr=(), eng="vector"):
        self.em.go(eng, "tensor_tensor", out, a, b, op=op, rd=rd, wr=wr)

    def ts(self, out, in0, s1, op0, s2=None, op1=None, rd=(), wr=(), eng="vector"):
        if s2 is None:
            self.em.go(eng, "tensor_scalar", out, in0, s1, None, op0,
                       rd=rd, wr=wr)
        else:
            self.em.go(eng, "tensor_scalar", out, in0, s1, s2, op0, op1,
                       rd=rd, wr=wr)

    def stt(self, out, in0, scalar, in1, op0, op1, rd=(), wr=(), eng="vector"):
        self.em.go(eng, "scalar_tensor_tensor", out, in0, scalar, in1, op0, op1,
                   rd=rd, wr=wr)

    def act(self, out, in_, f, scale=None, bias=None, rd=(), wr=()):
        kw = {}
        rd = list(rd)
        if scale is not None:
            kw["scale"] = scale
        if bias is not None:
            if isinstance(bias, (int, float)):
                h = self.ccol(bias)
                kw["bias"] = h[:, 0:1]
                rd.append(h)
            else:
                kw["bias"] = bias
        self.em.go("scalar", "activation", out, in_, f, rd=rd, wr=wr, **kw)

    def red(self, out, in_, op=ALU.add, rd=(), wr=(), eng="vector"):
        if op == ALU.add:
            self.em.go(eng, "reduce_sum", out, in_, AX.X, rd=rd, wr=wr)
        else:
            self.em.go(eng, "reduce_max", out, in_, AX.X, rd=rd, wr=wr)

    def cp(self, out, in_, rd=(), wr=(), eng="scalar"):
        if eng == "scalar":
            self.em.go("scalar", "copy", out, in_, rd=rd, wr=wr)
        else:
            self.em.go(eng, "tensor_copy", out, in_, rd=rd, wr=wr)

    def smul(self, out, in_, c, rd=(), wr=()):
        self.em.go("scalar", "mul", out, in_, c, rd=rd, wr=wr)

    def sadd(self, out, in_, c, rd=(), wr=()):
        self.em.go("scalar", "add", out, in_, c, rd=rd, wr=wr)

    def recip(self, out, in_, rd=(), wr=()):
        self.em.go("vector", "reciprocal", out, in_, rd=rd, wr=wr)

    def memset(self, ap, c, rd=(), wr=()):
        self.em.go("vector", "memset", ap, c, rd=rd, wr=wr)

    def scan(self, out, data0, data1, rd=(), wr=()):
        self.em.go("vector", "tensor_tensor_scan", out, data0, data1, 0.0,
                   ALU.mult, ALU.add, rd=rd, wr=wr)

    def sel(self, out, mask, on_true, on_false, rd=(), wr=()):
        self.em.go("vector", "select", out, mask, on_true, on_false,
                   rd=rd, wr=wr)

    def finish(self):
        sems = {}
        with ExitStack() as st:
            for e in ENGINES:
                sems[e] = st.enter_context(self.nc.semaphore(self._nm("sem")))
            with self.nc.Block() as block:
                self.em.finalize(block, sems)
            self.stack.close()
        return self.nc


def _ln_tile(b, out_sb, x_sb, sq_sb, m_sb, n, eps, extra_rd=()):
    rd = [x_sb] + list(extra_rd)
    b.red(m_sb[:, 0:1], x_sb[:, 0:n], rd=rd, wr=[m_sb])
    b.smul(m_sb[:, 0:1], m_sb[:, 0:1], 1.0 / n, rd=[m_sb], wr=[m_sb])
    b.act(sq_sb[:, 0:n], x_sb[:, 0:n], AF.Square, rd=rd, wr=[sq_sb])
    b.red(m_sb[:, 1:2], sq_sb[:, 0:n], rd=[sq_sb], wr=[m_sb])
    b.smul(m_sb[:, 1:2], m_sb[:, 1:2], 1.0 / n, rd=[m_sb], wr=[m_sb])
    b.act(m_sb[:, 2:3], m_sb[:, 0:1], AF.Square, rd=[m_sb], wr=[m_sb])
    b.tt(ALU.subtract, m_sb[:, 1:2], m_sb[:, 1:2], m_sb[:, 2:3], rd=[m_sb], wr=[m_sb])
    b.act(m_sb[:, 2:3], m_sb[:, 1:2], AF.Ln, bias=eps, rd=[m_sb], wr=[m_sb])
    b.act(m_sb[:, 3:4], m_sb[:, 2:3], AF.Exp, scale=-0.5, rd=[m_sb], wr=[m_sb])
    b.ts(out_sb[:, 0:n], x_sb[:, 0:n], m_sb[:, 0:1], ALU.subtract,
         s2=m_sb[:, 3:4], op1=ALU.mult, rd=rd + [m_sb], wr=[out_sb])


def _gelu_tanh(b, out_sb, x_ap, t1, t2, n, rd=(), wr=()):
    rd = list(rd)
    b.act(t1[:, 0:n], x_ap, AF.Square, rd=rd, wr=[t1])
    b.ts(t1[:, 0:n], t1[:, 0:n], 0.044715, ALU.mult, s2=1.0, op1=ALU.add,
         rd=[t1], wr=[t1])
    b.tt(ALU.mult, t2[:, 0:n], t1[:, 0:n], x_ap, rd=[t1] + rd, wr=[t2])
    b.act(t1[:, 0:n], t2[:, 0:n], AF.Tanh, scale=0.7978845608028654, rd=[t2], wr=[t1])
    b.ts(t1[:, 0:n], t1[:, 0:n], 1.0, ALU.add, rd=[t1], wr=[t1])
    b.tt(ALU.mult, t2[:, 0:n], t1[:, 0:n], x_ap, rd=[t1] + rd, wr=[t2])
    b.smul(out_sb, t2[:, 0:n], 0.5, rd=[t2], wr=list(wr))


def build_l0main():
    b = B("l0main")
    T0 = 4096
    x = b.inp("x", (T0, 1024))
    Wq = b.inp("Wq", (1024, 128))
    Wk = b.inp("Wk", (1024, 128))
    Wv = b.inp("Wv", (1024, 128))
    Wg = b.inp("Wg", (1024, 128))
    w1 = b.inp("w1", (1024, 8192))
    w2 = b.inp("w2", (8192, 1024))
    ew = b.inp("ew", (512, 2))
    I128 = b.inp("I128", (128, 128))
    maskI = b.inp("maskI", (128, 128), mybir.dt.int8)
    ones1 = b.inp("ones1", (1, 128))
    onesT = b.inp("onesT", (128, 128))
    o_h = b.outp("o_h", (T0, 128))
    gax = b.outp("gax", (512, 1024))

    ident = b.sb(128, 128)
    maskI_s = b.sb(128, 128, mybir.dt.int8)
    onesT_s = b.sb(128, 128)
    b.dma(ident[:, :], I128, wr=[ident])
    b.dma(maskI_s[:, :], maskI, wr=[maskI_s])
    b.dma(onesT_s[:, :], onesT, wr=[onesT_s])

    wts = {}
    wf0 = b.sb(128, 1024)
    for nm, ap in (("q", Wq), ("k", Wk), ("v", Wv), ("g", Wg)):
        wb = b.sb(128, 1024, BF16)
        b.dma(wf0[:, :], ap.rearrange("(a p) c -> p a c", p=128), rd=[wf0],
              wr=[wf0])
        b.cp(wb[:, :], wf0[:, :], rd=[wf0], wr=[wb])
        wts[nm] = wb

    xnTt = b.sb(128, 1024, BF16)
    qT = b.sb(128, T0)
    kT = b.sb(128, T0)
    krow = b.sb(128, T0)
    vrow = b.sb(128, T0)
    laT = b.sb(128, T0)

    xt = b.sb(128, 1024)
    xnr = b.sb(128, 1024)
    sq = b.sb(128, 1024)
    mc = b.sb(128, 8)
    pA = b.ps(128, 128)
    pB = b.ps(128, 128)
    pC = b.ps(128, 128)
    rowb = b.sb(128, 128)

    for tt in range(32):
        b.dma(xt[:, :], x[tt * 128:(tt + 1) * 128, :], rd=[xt], wr=[xt])
        _ln_tile(b, xnr, xt, sq, mc, 1024, 1e-5)
        for j in range(8):
            b.transp(pA[:, :], xnr[:, j * 128:(j + 1) * 128], ident[:, :],
                     rd=[xnr, ident], wr=[pA])
            b.cp(xnTt[:, j * 128:(j + 1) * 128], pA[:, :], rd=[pA], wr=[xnTt])
        for nm in ("q", "k", "v", "g"):
            for j in range(8):
                b.mm(pB[:, :], xnTt[:, j * 128:(j + 1) * 128],
                     wts[nm][:, j * 128:(j + 1) * 128], j == 0, j == 7,
                     rd=[xnTt, wts[nm]], wr=[pB])
            cs = slice(tt * 128, (tt + 1) * 128)
            if nm in ("q", "k"):
                b.act(sq[:, 0:128], pB[:, :], AF.Square, rd=[pB], wr=[sq])
                b.red(mc[:, 4:5], sq[:, 0:128], rd=[sq], wr=[mc])
                b.act(mc[:, 5:6], mc[:, 4:5], AF.Ln, bias=1e-24, rd=[mc], wr=[mc])
                b.act(mc[:, 6:7], mc[:, 5:6], AF.Exp, scale=-0.5, rd=[mc], wr=[mc])
                b.ts(rowb[:, :], pB[:, :], mc[:, 6:7], ALU.mult,
                     rd=[pB, mc], wr=[rowb])
                if nm == "k":
                    b.cp(krow[:, cs], rowb[:, :], rd=[rowb], wr=[krow], eng="vector")
                b.transp(pC[:, :], rowb[:, :], ident[:, :], rd=[rowb, ident], wr=[pC])
                b.cp((qT if nm == "q" else kT)[:, cs], pC[:, :], rd=[pC],
                     wr=[qT if nm == "q" else kT])
            elif nm == "v":
                b.cp(vrow[:, cs], pB[:, :], rd=[pB], wr=[vrow])
            else:
                b.act(sq[:, 0:128], pB[:, :], AF.Exp, scale=-1.0, rd=[pB], wr=[sq])
                b.act(sq[:, 128:256], sq[:, 0:128], AF.Ln, bias=1.0, rd=[sq], wr=[sq])
                b.smul(rowb[:, :], sq[:, 128:256], -1.0, rd=[sq], wr=[rowb])
                b.transp(pC[:, :], rowb[:, :], ident[:, :], rd=[rowb, ident], wr=[pC])
                b.cp(laT[:, cs], pC[:, :], rd=[pC], wr=[laT])

    xf = b.inp("xf", (512, 1024))
    xnTf = b.sb(128, 4 * 1024, BF16)
    for tt in range(4):
        b.dma(xt[:, :], xf[tt * 128:(tt + 1) * 128, :], rd=[xt], wr=[xt])
        _ln_tile(b, xnr, xt, sq, mc, 1024, 1e-5)
        for j in range(8):
            b.transp(pA[:, :], xnr[:, j * 128:(j + 1) * 128], ident[:, :],
                     rd=[xnr, ident], wr=[pA])
            b.cp(xnTf[:, j * 512 + tt * 128: j * 512 + (tt + 1) * 128],
                 pA[:, :], rd=[pA], wr=[xnTf])
    ewc = b.sb(128, 8)
    ew_sb = b.sb(128, 8)
    b.dma(ew_sb[:, 0:8], ew.rearrange("(a p) c -> p a c", p=128), wr=[ew_sb])
    for tt in range(4):
        b.tt(ALU.add, ewc[:, tt:tt + 1], ew_sb[:, 2 * tt:2 * tt + 1],
             ew_sb[:, 2 * tt + 1:2 * tt + 2], rd=[ew_sb], wr=[ewc])

    w1f = [b.sb(128, 1024) for _ in range(2)]
    w1b = [b.sb(128, 1024, BF16) for _ in range(2)]
    w2f = [b.sb(128, 1024) for _ in range(2)]
    w2b = [b.sb(128, 1024, BF16) for _ in range(2)]
    ph = b.ps(128, 512)
    py = [b.ps(128, 512) for _ in range(4)]
    hbf = b.sb(128, 512, BF16)
    g1 = b.sb(128, 512)
    g2 = b.sb(128, 512)
    yout = b.sb(128, 512)
    for half in range(2):
        for f in range(64):
            sl = f % 2
            b.dma(w1f[sl][:, :],
                  w1[:, f * 128:(f + 1) * 128].rearrange("(a p) c -> p a c", p=128),
                  rd=[w1b[sl]], wr=[w1f[sl]])
            b.cp(w1b[sl][:, :], w1f[sl][:, :], rd=[w1f[sl]], wr=[w1b[sl]])
            b.dma(w2f[sl][:, :], w2[f * 128:(f + 1) * 128, :], rd=[w2b[sl]],
                  wr=[w2f[sl]], q="gpsimd")
            b.cp(w2b[sl][:, :], w2f[sl][:, :], rd=[w2f[sl]], wr=[w2b[sl]],
                 eng="vector")
            for j in range(8):
                b.mm(ph[:, :], w1b[sl][:, j * 128:(j + 1) * 128],
                     xnTf[:, j * 512:(j + 1) * 512],
                     j == 0, j == 7, rd=[w1b[sl], xnTf], wr=[ph])
            _gelu_tanh(b, hbf[:, 0:512], ph[:, :], g1, g2, 512, rd=[ph], wr=[hbf])
            for tt in range(4):
                b.mm(py[tt][:, :], hbf[:, tt * 128:(tt + 1) * 128],
                     w2b[sl][:, half * 512:(half + 1) * 512], f == 0, f == 63,
                     rd=[hbf, w2b[sl]], wr=[py[tt]])
        for tt in range(4):
            b.ts(yout[:, :], py[tt][:, :], ewc[:, tt:tt + 1], ALU.mult,
                 rd=[py[tt], ewc], wr=[yout])
            b.act(yout[:, :], yout[:, :], AF.Silu, rd=[yout], wr=[yout])
            b.dma(gax[tt * 128:(tt + 1) * 128, half * 512:(half + 1) * 512],
                  yout[:, :], rd=[yout], q="gpsimd")

    S = b.sb(128, 128)
    S2 = b.sb(128, 128)
    zero128 = b.sb(128, 128)
    b.memset(zero128[:, :], 0.0, wr=[zero128])
    b.memset(S[:, :], 0.0, wr=[S])
    bT = b.sb(128, 128)
    tA = b.sb(128, 128)
    tD = b.sb(128, 128)
    expA = b.sb(128, 128)
    expB = b.sb(128, 128)
    expC = b.sb(128, 128)
    expD = b.sb(128, 128)
    q1 = b.sb(128, 128)
    k1 = b.sb(128, 128)
    q0 = b.sb(128, 128)
    k2 = b.sb(128, 128)
    eDr = b.sb(128, 128)
    PTm = b.sb(128, 128)
    Osb = b.sb(128, 128)
    hc = b.sb(128, 4)
    for c in range(32):
        cs = slice(c * 128, (c + 1) * 128)
        b.scan(bT[:, :], onesT_s[:, :], laT[:, cs], rd=[onesT_s, laT], wr=[bT])
        b.smul(hc[:, 0:1], bT[:, 127:128], 0.5, rd=[bT], wr=[hc])
        b.act(hc[:, 1:2], bT[:, 127:128], AF.Exp, rd=[bT], wr=[hc])
        b.ts(tA[:, :], bT[:, :], hc[:, 0:1], ALU.subtract, rd=[bT, hc], wr=[tA])
        b.act(expA[:, :], tA[:, :], AF.Exp, rd=[tA], wr=[expA])
        b.act(expB[:, :], tA[:, :], AF.Exp, scale=-1.0, rd=[tA], wr=[expB])
        b.act(expC[:, :], bT[:, :], AF.Exp, rd=[bT], wr=[expC])
        b.ts(tD[:, :], bT[:, :], hc[:, 0:1], ALU.subtract, s2=hc[:, 0:1],
             op1=ALU.subtract, rd=[bT, hc], wr=[tD])
        b.act(expD[:, :], tD[:, :], AF.Exp, scale=-1.0, rd=[tD], wr=[expD])
        b.tt(ALU.mult, q1[:, :], expA[:, :], qT[:, cs], rd=[expA, qT], wr=[q1])
        b.tt(ALU.mult, k1[:, :], expB[:, :], kT[:, cs], rd=[expB, kT], wr=[k1])
        b.tt(ALU.mult, q0[:, :], expC[:, :], qT[:, cs], rd=[expC, qT], wr=[q0])
        b.transp(pA[:, :], expD[:, :], ident[:, :], rd=[expD, ident], wr=[pA])
        b.cp(eDr[:, :], pA[:, :], rd=[pA], wr=[eDr])
        b.tt(ALU.mult, k2[:, :], eDr[:, :], krow[:, cs], rd=[eDr, krow], wr=[k2])
        b.mm(pB[:, :], k1[:, :], q1[:, :], True, True, rd=[k1, q1], wr=[pB])
        b.sel(PTm[:, :], maskI_s[:, :], pB[:, :], zero128[:, :],
              rd=[pB, maskI_s, zero128], wr=[PTm])
        b.mm(pC[:, :], PTm[:, :], vrow[:, cs], True, False, rd=[PTm, vrow], wr=[pC])
        b.mm(pC[:, :], q0[:, :], S[:, :], False, True, rd=[q0, S], wr=[pC])
        b.cp(Osb[:, :], pC[:, :], rd=[pC], wr=[Osb])
        b.dma(o_h[cs, :], Osb[:, :], rd=[Osb], q="gpsimd")
        b.mm(pA[:, :], k2[:, :], vrow[:, cs], True, True, rd=[k2, vrow], wr=[pA])
        b.stt(S2[:, :], S[:, :], hc[:, 1:2], pA[:, :], ALU.mult, ALU.add,
              rd=[S, hc, pA], wr=[S2])
        S, S2 = S2, S
    return b.finish()


def build_l0fin():
    b = B("l0fin")
    o = b.inp("o", (512, 1024))
    g = b.inp("g", (512, 1024))
    xs = b.inp("xs", (512, 1024))
    Wo = b.inp("Wo", (1024, 1024))
    I128 = b.inp("I128", (128, 128))
    x1o = b.outp("x1o", (512, 1024))

    ident = b.sb(128, 128)
    b.dma(ident[:, :], I128, wr=[ident])
    wf = b.sb(128, 8192)
    wb = b.sb(128, 8192, BF16)
    b.dma(wf[:, :], Wo.rearrange("(a p) c -> p a c", p=128), wr=[wf])
    b.cp(wb[:, :], wf[:, :], rd=[wf], wr=[wb])
    ot = b.sb(128, 1024)
    oT = b.sb(128, 1024, BF16)
    gt = b.sb(128, 1024)
    xt = b.sb(128, 1024)
    res = b.sb(128, 1024)
    pA = b.ps(128, 128)
    p0 = b.ps(128, 512)
    p1 = b.ps(128, 512)
    for tt in range(4):
        rs = slice(tt * 128, (tt + 1) * 128)
        b.dma(ot[:, :], o[rs, :], rd=[ot], wr=[ot])
        b.dma(gt[:, :], g[rs, :], rd=[gt], wr=[gt], q="gpsimd")
        b.dma(xt[:, :], xs[rs, :], rd=[xt], wr=[xt], q="gpsimd")
        for j in range(8):
            b.transp(pA[:, :], ot[:, j * 128:(j + 1) * 128], ident[:, :],
                     rd=[ot, ident], wr=[pA])
            b.cp(oT[:, j * 128:(j + 1) * 128], pA[:, :], rd=[pA], wr=[oT])
        for half, pp in ((0, p0), (1, p1)):
            for j in range(8):
                b.mm(pp[:, :], oT[:, j * 128:(j + 1) * 128],
                     wb[:, j * 1024 + half * 512: j * 1024 + (half + 1) * 512],
                     j == 0, j == 7, rd=[oT, wb], wr=[pp])
            hs = slice(half * 512, (half + 1) * 512)
            b.tt(ALU.mult, res[:, hs], pp[:, :], gt[:, hs], rd=[pp, gt], wr=[res])
            b.tt(ALU.add, res[:, hs], res[:, hs], xt[:, hs], rd=[res, xt], wr=[res])
            b.dma(x1o[rs, hs], res[:, hs], rd=[res], q="gpsimd")
    return b.finish()


def build_pre():
    b = B("pre")
    TL = 4096
    x = b.inp("x", (TL, 1024))
    prow = b.inp("prow", (1, 1024))
    Wr = b.inp("Wr", (1024, 1024))
    Wk = b.inp("Wk", (1024, 1024))
    Wv = b.inp("Wv", (1024, 1024))
    Wg = b.inp("Wg", (1024, 1024))
    maa = b.inp("maa", (1024, 7))
    mw1 = b.inp("mw1", (1024, 160))
    mw2 = b.inp("mw2", (160, 1024))
    dw1 = b.inp("dw1", (1024, 64))
    dw2 = b.inp("dw2", (64, 1024))
    I128 = b.inp("I128", (128, 128))
    rT_o = b.outp("rT_o", (1024, TL))
    kT_o = b.outp("kT_o", (1024, TL))
    wT_o = b.outp("wT_o", (1024, TL))
    krow_o = b.outp("krow_o", (TL, 1024))
    vrow_o = b.outp("vrow_o", (TL, 1024))
    g_o = b.outp("g_o", (TL, 1024))

    ident = b.sb(128, 128)
    b.dma(ident[:, :], I128, wr=[ident])
    maac = b.sb(128, 8 * 7)
    maaf = b.sb(128, 56)
    b.dma(maaf[:, :], maa.rearrange("(a p) c -> p a c", p=128), wr=[maaf])
    WB = {}
    wfs = b.sb(128, 8192)
    for nm, ap in (("r", Wr), ("k", Wk), ("v", Wv), ("g", Wg)):
        wb = b.sb(128, 8192, BF16)
        for j in range(8):
            b.dma(wfs[:, j * 1024:(j + 1) * 1024],
                  ap[j * 128:(j + 1) * 128, :], rd=[wfs], wr=[wfs])
        b.cp(wb[:, :], wfs[:, :], rd=[wfs], wr=[wb])
        WB[nm] = wb
    mw1b = b.sb(128, 8 * 160, BF16)
    b.dma(wfs[:, 0:1280], mw1.rearrange("(a p) c -> p a c", p=128), rd=[wfs],
          wr=[wfs])
    b.cp(mw1b[:, :], wfs[:, 0:1280], rd=[wfs], wr=[mw1b])
    mw2s = []
    for s in range(5):
        m2 = b.sb(32, 1024, BF16)
        b.dma(wfs[0:32, 0:1024], mw2[s * 32:(s + 1) * 32, :], rd=[wfs], wr=[wfs])
        b.cp(m2[0:32, :], wfs[0:32, 0:1024], rd=[wfs], wr=[m2])
        mw2s.append(m2)
    dw1b = b.sb(128, 8 * 64, BF16)
    b.dma(wfs[:, 0:512], dw1.rearrange("(a p) c -> p a c", p=128), rd=[wfs],
          wr=[wfs])
    b.cp(dw1b[:, :], wfs[:, 0:512], rd=[wfs], wr=[dw1b])
    dw2b = b.sb(64, 1024, BF16)
    b.dma(wfs[0:64, 0:1024], dw2, rd=[wfs], wr=[wfs])
    b.cp(dw2b[0:64, :], wfs[0:64, 0:1024], rd=[wfs], wr=[dw2b])

    xt = b.sb(128, 1024)
    xnr = b.sb(128, 1024)
    sq = b.sb(128, 1024)
    mc = b.sb(128, 8)
    prev = b.sb(1, 1024)
    prevn = b.sb(1, 1024)
    pm = b.sb(1, 8)
    b.dma(prev[:, :], prow, wr=[prev])
    b.red(pm[:, 0:1], prev[0:1, 0:1024], rd=[prev], wr=[pm])
    b.smul(pm[:, 0:1], pm[:, 0:1], 1.0 / 1024, rd=[pm], wr=[pm])
    b.act(prevn[0:1, :], prev[0:1, :], AF.Square, rd=[prev], wr=[prevn])
    b.red(pm[:, 1:2], prevn[0:1, :], rd=[prevn], wr=[pm])
    b.smul(pm[:, 1:2], pm[:, 1:2], 1.0 / 1024, rd=[pm], wr=[pm])
    b.act(pm[:, 2:3], pm[:, 0:1], AF.Square, rd=[pm], wr=[pm])
    b.tt(ALU.subtract, pm[:, 1:2], pm[:, 1:2], pm[:, 2:3], rd=[pm], wr=[pm])
    eps1 = b.sb(1, 1)
    b.memset(eps1[:, 0:1], 1e-5, wr=[eps1])
    b.tt(ALU.add, pm[:, 1:2], pm[:, 1:2], eps1[0:1, 0:1], rd=[pm, eps1], wr=[pm])
    b.act(pm[:, 2:3], pm[:, 1:2], AF.Ln, rd=[pm], wr=[pm])
    b.act(pm[:, 3:4], pm[:, 2:3], AF.Exp, scale=-0.5, rd=[pm], wr=[pm])
    b.ts(prevn[0:1, :], prev[0:1, :], pm[:, 0:1], ALU.subtract, s2=pm[:, 3:4],
         op1=ALU.mult, rd=[prev, pm], wr=[prevn])

    NT = 8
    xTf = b.sb(128, 8 * 512)
    xxT = b.sb(128, 8 * 512)
    xxxT = b.sb(128, 8 * 512, BF16)
    lastc = b.sb(128, 8)
    mixT = [b.sb(32, 512, BF16) for _ in range(5)]
    xsT = b.sb(128, 8 * 512, BF16)
    pA = b.ps(128, 512)
    pB = b.ps(128, 512)
    pC = b.ps(128, 128)
    mid = b.sb(64, 512, BF16)
    outs = b.sb(128, 512)

    for j in range(8):
        b.transp(pC[:, 0:1], prevn[0:1, j * 128:(j + 1) * 128], ident[0:1, 0:1],
                 rd=[prevn, ident], wr=[pC])
        b.cp(lastc[:, j:j + 1], pC[:, 0:1], rd=[pC], wr=[lastc])

    for mt in range(NT):
        for q in range(4):
            tt = mt * 4 + q
            b.dma(xt[:, :], x[tt * 128:(tt + 1) * 128, :], rd=[xt], wr=[xt])
            _ln_tile(b, xnr, xt, sq, mc, 1024, 1e-5)
            for j in range(8):
                b.transp(pC[:, :], xnr[:, j * 128:(j + 1) * 128], ident[:, :],
                         rd=[xnr, ident], wr=[pC])
                b.cp(xTf[:, j * 512 + q * 128: j * 512 + (q + 1) * 128],
                     pC[:, :], rd=[pC], wr=[xTf])
        for j in range(8):
            js = slice(j * 512, (j + 1) * 512)
            b.tt(ALU.subtract, xxT[:, j * 512 + 1: (j + 1) * 512],
                 xTf[:, j * 512: (j + 1) * 512 - 1],
                 xTf[:, j * 512 + 1: (j + 1) * 512], rd=[xTf], wr=[xxT])
            b.tt(ALU.subtract, xxT[:, j * 512: j * 512 + 1],
                 lastc[:, j:j + 1], xTf[:, j * 512: j * 512 + 1],
                 rd=[lastc, xTf], wr=[xxT])
            b.cp(lastc[:, j:j + 1], xTf[:, (j + 1) * 512 - 1:(j + 1) * 512],
                 rd=[xTf], wr=[lastc])
        for j in range(8):
            js = slice(j * 512, (j + 1) * 512)
            b.stt(xxxT[:, js], xxT[:, js], maaf[:, j * 7 + 0: j * 7 + 1],
                  xTf[:, js], ALU.mult, ALU.add, rd=[xxT, maaf, xTf], wr=[xxxT])
        for s in range(5):
            for j in range(8):
                b.mm(pA[0:32, :], mw1b[:, j * 160 + s * 32: j * 160 + (s + 1) * 32],
                     xxxT[:, j * 512:(j + 1) * 512], j == 0, j == 7,
                     rd=[mw1b, xxxT], wr=[pA])
            b.act(mixT[s][0:32, :], pA[0:32, :], AF.Tanh, rd=[pA], wr=[mixT[s]])
        tgts = [("w", 1), ("k", 2), ("v", 3), ("r", 4), ("g", 5)]
        for s, (nm, mcol) in enumerate(tgts):
            for j in range(8):
                b.mm(pA[:, :], mw2s[s][0:32, j * 128:(j + 1) * 128],
                     mixT[s][0:32, :], True, True, rd=[mw2s[s], mixT[s]], wr=[pA])
                js = slice(j * 512, (j + 1) * 512)
                b.ts(outs[:, :], pA[:, :], maaf[:, j * 7 + mcol: j * 7 + mcol + 1],
                     ALU.add, rd=[pA, maaf], wr=[outs])
                b.tt(ALU.mult, outs[:, :], outs[:, :], xxT[:, js],
                     rd=[outs, xxT], wr=[outs])
                b.tt(ALU.add, xsT[:, js], outs[:, :], xTf[:, js],
                     rd=[outs, xTf], wr=[xsT])
            ts_cols = slice(mt * 512, (mt + 1) * 512)
            if nm == "w":
                for j in range(8):
                    b.mm(pA[0:64, :], dw1b[:, j * 64:(j + 1) * 64],
                         xsT[:, j * 512:(j + 1) * 512], j == 0, j == 7,
                         rd=[dw1b, xsT], wr=[pA])
                b.act(mid[0:64, :], pA[0:64, :], AF.Tanh, rd=[pA], wr=[mid])
                for cj in range(8):
                    b.mm(pB[:, :], dw2b[0:64, cj * 128:(cj + 1) * 128],
                         mid[0:64, :], True, True, rd=[dw2b, mid], wr=[pB])
                    b.ts(outs[:, :], pB[:, :], maaf[:, cj * 7 + 6: cj * 7 + 7],
                         ALU.add, rd=[pB, maaf], wr=[outs])
                    b.dma(wT_o[cj * 128:(cj + 1) * 128, ts_cols], outs[:, :],
                          rd=[outs], q="gpsimd")
            elif nm in ("r", "k"):
                dst = rT_o if nm == "r" else kT_o
                for cj in range(8):
                    for j in range(8):
                        b.mm(pB[:, :], WB[nm][:, j * 1024 + cj * 128: j * 1024 + (cj + 1) * 128],
                             xsT[:, j * 512:(j + 1) * 512], j == 0, j == 7,
                             rd=[WB[nm], xsT], wr=[pB])
                    b.cp(outs[:, :], pB[:, :], rd=[pB], wr=[outs])
                    b.dma(dst[cj * 128:(cj + 1) * 128, ts_cols], outs[:, :],
                          rd=[outs], q="gpsimd")
                if nm == "k":
                    for q in range(4):
                        for half in range(2):
                            pp = pA if half == 0 else pB
                            for j in range(8):
                                b.mm(pp[:, :],
                                     xsT[:, j * 512 + q * 128: j * 512 + (q + 1) * 128],
                                     WB[nm][:, j * 1024 + half * 512: j * 1024 + (half + 1) * 512],
                                     j == 0, j == 7, rd=[xsT, WB[nm]], wr=[pp])
                            b.cp(outs[:, :], pp[:, :], rd=[pp], wr=[outs])
                            b.dma(krow_o[(mt * 4 + q) * 128:(mt * 4 + q + 1) * 128,
                                         half * 512:(half + 1) * 512],
                                  outs[:, :], rd=[outs], q="gpsimd")
            else:
                dst = vrow_o if nm == "v" else g_o
                for q in range(4):
                    for half in range(2):
                        pp = pA if half == 0 else pB
                        for j in range(8):
                            b.mm(pp[:, :],
                                 xsT[:, j * 512 + q * 128: j * 512 + (q + 1) * 128],
                                 WB[nm][:, j * 1024 + half * 512: j * 1024 + (half + 1) * 512],
                                 j == 0, j == 7, rd=[xsT, WB[nm]], wr=[pp])
                        if nm == "g":
                            b.act(outs[:, :], pp[:, :], AF.Silu, rd=[pp], wr=[outs])
                        else:
                            b.cp(outs[:, :], pp[:, :], rd=[pp], wr=[outs])
                        b.dma(dst[(mt * 4 + q) * 128:(mt * 4 + q + 1) * 128,
                                  half * 512:(half + 1) * 512],
                              outs[:, :], rd=[outs], q="gpsimd")
    return b.finish()


def build_wkv():
    b = B("wkv")
    T = 8192
    rT = b.inp("rT", (128, 4 * T))
    kT = b.inp("kT", (128, 4 * T))
    wT = b.inp("wT", (128, 4 * T))
    krow = b.inp("krow", (4 * T, 128))
    vrow = b.inp("vrow", (4 * T, 128))
    ucol = b.inp("ucol", (128, 1))
    I128 = b.inp("I128", (128, 128))
    maskS = b.inp("maskS", (128, 128), mybir.dt.int8)
    ones2 = b.inp("ones2", (128, 2))
    rmask = b.inp("rmask", (128, 512))
    onesF = b.inp("onesF", (128, 128))
    shiftM = b.inp("shiftM", (128, 64))
    y_o = b.outp("y_o", (4 * T, 128))

    ident = b.sb(128, 128)
    maskS_s = b.sb(128, 128, mybir.dt.int8)
    ones2_s = b.sb(128, 2)
    rmask_s = b.sb(128, 512)
    onesF_s = b.sb(128, 128)
    uc = b.sb(128, 1)
    shf = b.sb(128, 64)
    for dst, src in ((ident, I128), (maskS_s, maskS), (ones2_s, ones2),
                     (rmask_s, rmask), (onesF_s, onesF), (uc, ucol),
                     (shf, shiftM)):
        b.dma(dst[:, :], src, wr=[dst])

    rTs = [b.sb(128, 512) for _ in range(2)]
    kTs = [b.sb(128, 512) for _ in range(2)]
    wTs = [b.sb(128, 512) for _ in range(2)]
    krs = [b.sb(128, 512) for _ in range(2)]
    vrs = [b.sb(128, 512) for _ in range(2)]
    la = b.sb(128, 512)
    lan = b.sb(128, 512)
    bT = b.sb(128, 512)
    eT = b.sb(128, 512)
    hTl = b.sb(128, 512)
    tA = b.sb(128, 512)
    tB = b.sb(128, 512)
    tD = b.sb(128, 512)
    q1 = b.sb(128, 512)
    k1 = b.sb(128, 512)
    q0 = b.sb(128, 512)
    k2 = b.sb(128, 512)
    eDr = b.sb(128, 512)
    rku = b.sb(128, 512)
    hcol = b.sb(128, 8)
    ebl = b.sb(128, 4)
    eblh = b.sb(64, 4)
    q0lo = b.sb(64, 128)
    drk = b.sb(128, 8)
    d2 = b.sb(2, 512)
    PTm = b.sb(128, 128)
    Osb = b.sb(128, 128)
    zero128 = b.sb(128, 128)
    b.memset(zero128[:, :], 0.0, wr=[zero128])
    S = b.sb(64, 512)
    S2 = b.sb(64, 512)
    b.memset(S[:, :], 0.0, wr=[S])
    b.memset(S2[:, :], 0.0, wr=[S2])
    pA = b.ps(128, 128)
    pB = b.ps(128, 128)
    pC = b.ps(128, 128)
    pD = b.ps(128, 512)

    def bcols(bi, ci):
        t0 = bi * T + ci * 128
        return slice(t0, t0 + 128)

    for ci in range(64):
        sl = ci % 2
        for nm, buf, src in (("r", rTs[sl], rT), ("k", kTs[sl], kT),
                             ("w", wTs[sl], wT)):
            ap3 = src.rearrange("p (a t) -> p a t", t=T)[:, :, ci * 128:(ci + 1) * 128]
            b.dma(buf[:, :], ap3, rd=[buf], wr=[buf])
        for bi in range(4):
            b.dma(krs[sl][:, bi * 128:(bi + 1) * 128],
                  krow[bi * T + ci * 128: bi * T + (ci + 1) * 128, :],
                  rd=[krs[sl]], wr=[krs[sl]], q="gpsimd")
            b.dma(vrs[sl][:, bi * 128:(bi + 1) * 128],
                  vrow[bi * T + ci * 128: bi * T + (ci + 1) * 128, :],
                  rd=[vrs[sl]], wr=[vrs[sl]], q="gpsimd")
        rTc, kTc, wTc, krc, vrc = rTs[sl], kTs[sl], wTs[sl], krs[sl], vrs[sl]
        b.act(la[:, :], wTc[:, :], AF.Exp, rd=[wTc], wr=[la])
        b.smul(lan[:, :], la[:, :], -1.0, rd=[la], wr=[lan])
        b.scan(bT[:, :], rmask_s[:, :], lan[:, :], rd=[rmask_s, lan], wr=[bT])
        b.tt(ALU.subtract, eT[:, :], bT[:, :], lan[:, :], rd=[bT, lan], wr=[eT])
        for bi in range(4):
            b.smul(hcol[:, bi:bi + 1], bT[:, bi * 128 + 127: bi * 128 + 128],
                   0.5, rd=[bT], wr=[hcol])
            b.act(ebl[:, bi:bi + 1], bT[:, bi * 128 + 127: bi * 128 + 128],
                  AF.Exp, rd=[bT], wr=[ebl])
            b.ts(hTl[:, bi * 128:(bi + 1) * 128], onesF_s[:, :],
                 hcol[:, bi:bi + 1], ALU.mult, rd=[onesF_s, hcol], wr=[hTl])
        b.tt(ALU.subtract, tA[:, :], eT[:, :], hTl[:, :], rd=[eT, hTl], wr=[tA])
        b.act(tA[:, :], tA[:, :], AF.Exp, rd=[tA], wr=[tA])
        b.tt(ALU.mult, q1[:, :], tA[:, :], rTc[:, :], rd=[tA, rTc], wr=[q1])
        b.tt(ALU.subtract, tB[:, :], bT[:, :], hTl[:, :], rd=[bT, hTl], wr=[tB])
        b.tt(ALU.subtract, tD[:, :], tB[:, :], hTl[:, :], rd=[tB, hTl], wr=[tD])
        b.act(tB[:, :], tB[:, :], AF.Exp, scale=-1.0, rd=[tB], wr=[tB])
        b.tt(ALU.mult, k1[:, :], tB[:, :], kTc[:, :], rd=[tB, kTc], wr=[k1])
        b.act(eT[:, :], eT[:, :], AF.Exp, rd=[eT], wr=[eT])
        b.tt(ALU.mult, q0[:, :], eT[:, :], rTc[:, :], rd=[eT, rTc], wr=[q0])
        b.act(tD[:, :], tD[:, :], AF.Exp, scale=-1.0, rd=[tD], wr=[tD])
        for bi in range(4):
            b.transp(pA[:, :], tD[:, bi * 128:(bi + 1) * 128], ident[:, :],
                     rd=[tD, ident], wr=[pA])
            b.cp(eDr[:, bi * 128:(bi + 1) * 128], pA[:, :], rd=[pA], wr=[eDr])
        b.tt(ALU.mult, k2[:, :], eDr[:, :], krc[:, :], rd=[eDr, krc], wr=[k2])
        b.tt(ALU.mult, rku[:, :], rTc[:, :], kTc[:, :], rd=[rTc, kTc], wr=[rku])
        b.ts(rku[:, :], rku[:, :], uc[:, 0:1], ALU.mult, rd=[rku, uc], wr=[rku])
        for bi in range(4):
            b.mm(pD[0:2, bi * 128:(bi + 1) * 128], ones2_s[:, 0:2],
                 rku[:, bi * 128:(bi + 1) * 128], True, True,
                 rd=[ones2_s, rku], wr=[pD])
        b.cp(d2[0:2, :], pD[0:2, :], rd=[pD], wr=[d2])
        for bi in range(4):
            b.transp(pA[0:128, 0:2], d2[0:2, bi * 128:(bi + 1) * 128],
                     ident[0:2, 0:2], rd=[d2, ident], wr=[pA])
            b.cp(drk[:, bi * 2: bi * 2 + 2], pA[:, 0:2], rd=[pA], wr=[drk])
        b.mm(pD[0:64, 0:4], shf[:, :], ebl[:, 0:4], True, True,
             rd=[shf, ebl], wr=[pD])
        b.cp(eblh[0:64, 0:4], pD[0:64, 0:4], rd=[pD], wr=[eblh])
        for bi in range(4):
            bc = slice(bi * 128, (bi + 1) * 128)
            b.mm(pD[0:64, 128:256], shf[:, :], q0[:, bc], True, True,
                 rd=[shf, q0], wr=[pD])
            b.cp(q0lo[0:64, :], pD[0:64, 128:256], rd=[pD], wr=[q0lo])
            for h in range(2):
                hr = slice(64 * h, 64 * h + 64)
                blk = slice((bi * 2 + h) * 64, (bi * 2 + h) * 64 + 64)
                vsl = slice(bi * 128 + 64 * h, bi * 128 + 64 * h + 64)
                b.mm(pA[:, :], k1[hr, bc], q1[hr, bc], True, True,
                     rd=[k1, q1], wr=[pA])
                b.sel(PTm[:, :], maskS_s[:, :], pA[:, :], zero128[:, :],
                      rd=[pA, maskS_s, zero128], wr=[PTm])
                b.mm(pB[:, 0:64], PTm[:, :], vrc[:, vsl], True, False,
                     rd=[PTm, vrc], wr=[pB])
                lhs_q0 = q0[0:64, bc] if h == 0 else q0lo[0:64, :]
                b.mm(pB[:, 0:64], lhs_q0, S[0:64, blk], False, True,
                     rd=[q0, q0lo, S], wr=[pB])
                b.stt(Osb[:, 64 * h:64 * h + 64], vrc[:, vsl],
                      drk[:, bi * 2 + h: bi * 2 + h + 1], pB[:, 0:64],
                      ALU.mult, ALU.add, rd=[vrc, drk, pB], wr=[Osb])
                b.mm(pC[0:64, 0:64], k2[:, vsl], vrc[:, vsl], True, True,
                     rd=[k2, vrc], wr=[pC])
                scal = ebl[0:64, bi:bi + 1] if h == 0 else eblh[0:64, bi:bi + 1]
                b.stt(S2[0:64, blk], S[0:64, blk], scal,
                      pC[0:64, 0:64], ALU.mult, ALU.add,
                      rd=[S, ebl, eblh, pC], wr=[S2])
            b.dma(y_o[bi * T + ci * 128: bi * T + (ci + 1) * 128, :],
                  Osb[:, :], rd=[Osb], q="gpsimd")
        S, S2 = S2, S
    return b.finish()


def build_post():
    b = B("post")
    TL = 4096
    x = b.inp("x", (TL, 1024))
    y = b.inp("y", (TL, 1024))
    g = b.inp("g", (TL, 1024))
    lnw = b.inp("lnw", (1, 1024))
    lnb = b.inp("lnb", (1, 1024))
    Wo = b.inp("Wo", (1024, 1024))
    ipw = b.inp("ipw", (3072, 1024))
    ipb = b.inp("ipb", (1, 3072))
    opw = b.inp("opw", (1024, 1024))
    opb = b.inp("opb", (1, 1024))
    l1g = b.inp("l1g", (1, 1024))
    l1b = b.inp("l1b", (1, 1024))
    wg = b.inp("wg", (1024, 8))
    I128 = b.inp("I128", (128, 128))
    ones1 = b.inp("ones1", (1, 128))
    s_o = b.outp("s_o", (TL, 1024))
    y1_o = b.outp("y1_o", (TL, 1024))
    m8_o = b.outp("m8_o", (TL, 8))
    gv_o = b.outp("gv_o", (TL, 1))

    ident = b.sb(128, 128)
    b.dma(ident[:, :], I128, wr=[ident])
    ones1_s = b.sb(1, 128)
    b.dma(ones1_s[:, :], ones1, wr=[ones1_s])
    pA = b.ps(128, 512)
    pB = b.ps(128, 512)
    pC = b.ps(128, 128)

    def bcast(dst_sb, row_ap, n, rdx=()):
        for c0 in range(0, n, 512):
            w = min(512, n - c0)
            b.mm(pA[:, 0:w], ones1_s[0:1, :], row_ap[0:1, c0:c0 + w], True, True,
                 rd=[ones1_s] + list(rdx), wr=[pA])
            b.cp(dst_sb[:, c0:c0 + w], pA[:, 0:w], rd=[pA], wr=[dst_sb])

    rowf = b.sb(1, 3072)
    wtmp = b.sb(128, 1024)
    xt = b.sb(128, 1024)
    xnr = b.sb(128, 1024)
    sq = b.sb(128, 1024)
    mc = b.sb(128, 8)
    b.ccol(1e-5)
    b.ccol(6.4e-4)

    st1 = ExitStack()
    lnw_b = b.sbx(st1, 128, 1024)
    lnb_b = b.sbx(st1, 128, 1024)
    for dst, src in ((lnw_b, lnw), (lnb_b, lnb)):
        b.dma(rowf[0:1, 0:1024], src, rd=[rowf], wr=[rowf])
        bcast(dst, rowf, 1024, rdx=[rowf])
    wob = b.sbx(st1, 128, 8192, BF16)
    for j in range(8):
        b.dma(wtmp[:, :], Wo[j * 128:(j + 1) * 128, :], rd=[wtmp], wr=[wtmp])
        b.cp(wob[:, j * 1024:(j + 1) * 1024], wtmp[:, :], rd=[wtmp], wr=[wob])
    yt = b.sbx(st1, 128, 1024)
    gn = b.sbx(st1, 128, 48)
    zt = b.sbx(st1, 128, 1024)
    zT = b.sbx(st1, 128, 1024, BF16)
    souts = b.sbx(st1, 128, 1024)

    for tt in range(32):
        rs = slice(tt * 128, (tt + 1) * 128)
        b.dma(yt[:, :], y[rs, :], rd=[yt], wr=[yt])
        b.dma(xt[:, :], g[rs, :], rd=[xt], wr=[xt], q="gpsimd")
        for h in range(16):
            hs = slice(h * 64, (h + 1) * 64)
            b.red(gn[:, h:h + 1], yt[:, hs], rd=[yt], wr=[gn])
            b.smul(gn[:, h:h + 1], gn[:, h:h + 1], 1.0 / 64, rd=[gn], wr=[gn])
        b.act(sq[:, :], yt[:, :], AF.Square, rd=[yt], wr=[sq])
        for h in range(16):
            hs = slice(h * 64, (h + 1) * 64)
            b.red(gn[:, 16 + h:17 + h], sq[:, hs], rd=[sq], wr=[gn])
            b.smul(gn[:, 16 + h:17 + h], gn[:, 16 + h:17 + h], 1.0 / 64,
                   rd=[gn], wr=[gn])
        b.act(gn[:, 32:48], gn[:, 0:16], AF.Square, rd=[gn], wr=[gn])
        b.tt(ALU.subtract, gn[:, 16:32], gn[:, 16:32], gn[:, 32:48],
             rd=[gn], wr=[gn])
        b.act(gn[:, 32:48], gn[:, 16:32], AF.Ln, bias=6.4e-4, rd=[gn], wr=[gn])
        b.act(gn[:, 32:48], gn[:, 32:48], AF.Exp, scale=-0.5, rd=[gn], wr=[gn])
        for h in range(16):
            hs = slice(h * 64, (h + 1) * 64)
            b.ts(zt[:, hs], yt[:, hs], gn[:, h:h + 1], ALU.subtract,
                 s2=gn[:, 32 + h:33 + h], op1=ALU.mult, rd=[yt, gn], wr=[zt])
        b.tt(ALU.mult, zt[:, :], zt[:, :], lnw_b[:, :], rd=[zt, lnw_b], wr=[zt])
        b.tt(ALU.add, zt[:, :], zt[:, :], lnb_b[:, :], rd=[zt, lnb_b], wr=[zt])
        b.tt(ALU.mult, zt[:, :], zt[:, :], xt[:, :], rd=[zt, xt], wr=[zt])
        for j in range(8):
            b.transp(pC[:, :], zt[:, j * 128:(j + 1) * 128], ident[:, :],
                     rd=[zt, ident], wr=[pC])
            b.cp(zT[:, j * 128:(j + 1) * 128], pC[:, :], rd=[pC], wr=[zT])
        for half in range(2):
            for j in range(8):
                b.mm(pA[:, :], zT[:, j * 128:(j + 1) * 128],
                     wob[:, j * 1024 + half * 512: j * 1024 + (half + 1) * 512],
                     j == 0, j == 7, rd=[zT, wob], wr=[pA])
            b.cp(souts[:, half * 512:(half + 1) * 512], pA[:, :], rd=[pA],
                 wr=[souts])
        b.dma(s_o[rs, :], souts[:, :], rd=[souts], q="gpsimd")

    b.em.barrier()
    st1.close()

    ipb_b = b.sb(128, 3072)
    opb_b = b.sb(128, 1024)
    l1g_b = b.sb(128, 1024)
    l1b_b = b.sb(128, 1024)
    for dst, src, n in ((opb_b, opb, 1024), (l1g_b, l1g, 1024),
                        (l1b_b, l1b, 1024)):
        b.dma(rowf[0:1, 0:n], src, rd=[rowf], wr=[rowf])
        bcast(dst, rowf, n, rdx=[rowf])
    b.dma(rowf[0:1, 0:3072], ipb, rd=[rowf], wr=[rowf])
    bcast(ipb_b, rowf, 3072, rdx=[rowf])
    wgf = b.sb(128, 64)
    b.dma(wgf[:, :], wg.rearrange("(a p) c -> p a c", p=128), wr=[wgf])
    WTin = b.sb(128, 8 * 3072, BF16)
    for jj in range(24):
        b.dma(wtmp[:, :], ipw[jj * 128:(jj + 1) * 128, :], rd=[wtmp], wr=[wtmp])
        for j in range(8):
            b.transp(pC[:, :], wtmp[:, j * 128:(j + 1) * 128], ident[:, :],
                     rd=[wtmp, ident], wr=[pC])
            b.cp(WTin[:, j * 3072 + jj * 128: j * 3072 + (jj + 1) * 128],
                 pC[:, :], rd=[pC], wr=[WTin])
    WTout = b.sb(128, 8 * 1024, BF16)
    for jj in range(8):
        b.dma(wtmp[:, :], opw[jj * 128:(jj + 1) * 128, :], rd=[wtmp], wr=[wtmp])
        for j in range(8):
            b.transp(pC[:, :], wtmp[:, j * 128:(j + 1) * 128], ident[:, :],
                     rd=[wtmp, ident], wr=[pC])
            b.cp(WTout[:, j * 1024 + jj * 128: j * 1024 + (jj + 1) * 128],
                 pC[:, :], rd=[pC], wr=[WTout])

    xnT4 = b.sb(128, 4 * 1024, BF16)
    qkv = [b.sb(128, 3072, BF16) for _ in range(4)]
    xnl = [b.sb(128, 1024) for _ in range(4)]
    sc = b.sb(128, 64)
    att = b.sb(128, 64)
    dnm = b.sb(128, 16)
    acc = b.sb(128, 1024)
    oT = b.sb(128, 1024, BF16)
    y1t = b.sb(128, 1024)
    y1T = b.sb(128, 1024)
    g8 = b.sb(128, 8)
    gvc = b.sb(128, 1)
    for nt in range(8):
        for l in range(4):
            rs = slice(l * 1024 + nt * 128, l * 1024 + (nt + 1) * 128)
            b.dma(xt[:, :], x[rs, :], rd=[xt], wr=[xt])
            _ln_tile(b, xnl[l], xt, sq, mc, 1024, 1e-5)
            for j in range(8):
                b.transp(pC[:, :], xnl[l][:, j * 128:(j + 1) * 128], ident[:, :],
                         rd=[xnl[l], ident], wr=[pC])
                b.cp(xnT4[:, l * 1024 + j * 128: l * 1024 + (j + 1) * 128],
                     pC[:, :], rd=[pC], wr=[xnT4])
            for c6 in range(6):
                for j in range(8):
                    b.mm(pA[:, :], xnT4[:, l * 1024 + j * 128: l * 1024 + (j + 1) * 128],
                         WTin[:, j * 3072 + c6 * 512: j * 3072 + (c6 + 1) * 512],
                         j == 0, j == 7, rd=[xnT4, WTin], wr=[pA])
                cs6 = slice(c6 * 512, (c6 + 1) * 512)
                b.tt(ALU.add, sq[:, 0:512], pA[:, :], ipb_b[:, cs6],
                     rd=[pA, ipb_b], wr=[sq])
                b.cp(qkv[l][:, cs6], sq[:, 0:512], rd=[sq], wr=[qkv[l]])
        for l in range(4):
            for m in range(4):
                b.tt(ALU.mult, sq[:, :], qkv[l][:, 0:1024],
                     qkv[m][:, 1024:2048], rd=[qkv[l], qkv[m]], wr=[sq])
                b.red(sc[:, l * 16 + m * 4: l * 16 + m * 4 + 4],
                      sq[:, :].rearrange("p (h d) -> p h d", d=256),
                      rd=[sq], wr=[sc])
        b.smul(sc[:, :], sc[:, :], 1.0 / 16.0, rd=[sc], wr=[sc])
        b.act(sc[:, :], sc[:, :], AF.Exp, rd=[sc], wr=[sc])
        for l in range(4):
            ds = slice(l * 4, (l + 1) * 4)
            b.tt(ALU.add, dnm[:, ds], sc[:, l * 16: l * 16 + 4],
                 sc[:, l * 16 + 4: l * 16 + 8], rd=[sc], wr=[dnm])
            b.tt(ALU.add, dnm[:, ds], dnm[:, ds],
                 sc[:, l * 16 + 8: l * 16 + 12], rd=[sc, dnm], wr=[dnm])
            b.tt(ALU.add, dnm[:, ds], dnm[:, ds],
                 sc[:, l * 16 + 12: l * 16 + 16], rd=[sc, dnm], wr=[dnm])
        b.recip(dnm[:, 0:16], dnm[:, 0:16], rd=[dnm], wr=[dnm])
        for l in range(4):
            for m in range(4):
                b.tt(ALU.mult, att[:, l * 16 + m * 4: l * 16 + m * 4 + 4],
                     sc[:, l * 16 + m * 4: l * 16 + m * 4 + 4],
                     dnm[:, l * 4:(l + 1) * 4], rd=[sc, dnm], wr=[att])
        for l in range(4):
            for h in range(4):
                hs = slice(2048 + h * 256, 2048 + (h + 1) * 256)
                os = slice(h * 256, (h + 1) * 256)
                for m in range(4):
                    col = att[:, l * 16 + m * 4 + h: l * 16 + m * 4 + h + 1]
                    if m == 0:
                        b.ts(acc[:, os], qkv[m][:, hs], col, ALU.mult,
                             rd=[qkv[m], att], wr=[acc])
                    else:
                        b.stt(acc[:, os], qkv[m][:, hs], col, acc[:, os],
                              ALU.mult, ALU.add, rd=[qkv[m], att, acc], wr=[acc])
            for j in range(8):
                b.transp(pC[:, :], acc[:, j * 128:(j + 1) * 128], ident[:, :],
                         rd=[acc, ident], wr=[pC])
                b.cp(oT[:, j * 128:(j + 1) * 128], pC[:, :], rd=[pC], wr=[oT])
            for half in range(2):
                for j in range(8):
                    b.mm(pA[:, :], oT[:, j * 128:(j + 1) * 128],
                         WTout[:, j * 1024 + half * 512: j * 1024 + (half + 1) * 512],
                         j == 0, j == 7, rd=[oT, WTout], wr=[pA])
                hs2 = slice(half * 512, (half + 1) * 512)
                b.tt(ALU.add, y1t[:, hs2], pA[:, :], opb_b[:, hs2],
                     rd=[pA, opb_b], wr=[y1t])
            b.tt(ALU.add, y1t[:, :], y1t[:, :], xnl[l][:, :], rd=[y1t, xnl[l]],
                 wr=[y1t])
            _ln_tile(b, y1t, y1t, sq, mc, 1024, 1e-5)
            b.tt(ALU.mult, y1t[:, :], y1t[:, :], l1g_b[:, :], rd=[y1t, l1g_b],
                 wr=[y1t])
            b.tt(ALU.add, y1t[:, :], y1t[:, :], l1b_b[:, :], rd=[y1t, l1b_b],
                 wr=[y1t])
            rs = slice(l * 1024 + nt * 128, l * 1024 + (nt + 1) * 128)
            b.dma(y1_o[rs, :], y1t[:, :], rd=[y1t], q="gpsimd")
            for j in range(8):
                b.transp(pC[:, :], y1t[:, j * 128:(j + 1) * 128], ident[:, :],
                         rd=[y1t, ident], wr=[pC])
                b.cp(y1T[:, j * 128:(j + 1) * 128], pC[:, :], rd=[pC], wr=[y1T])
            for j in range(8):
                b.mm(pA[:, 0:8], y1T[:, j * 128:(j + 1) * 128],
                     wgf[:, j * 8:(j + 1) * 8], j == 0, j == 7,
                     rd=[y1T, wgf], wr=[pA])
            b.act(g8[:, :], pA[:, 0:8], AF.Exp, rd=[pA], wr=[g8])
            b.red(gvc[:, 0:1], g8[:, :], rd=[g8], wr=[gvc])
            b.recip(gvc[:, 0:1], gvc[:, 0:1], rd=[gvc], wr=[gvc])
            b.ts(g8[:, :], g8[:, :], gvc[:, 0:1], ALU.mult, rd=[g8, gvc], wr=[g8])
            b.red(gvc[:, 0:1], g8[:, :], op=ALU.max, rd=[g8], wr=[gvc])
            b.dma(gv_o[rs, :], gvc[:, 0:1], rd=[gvc], q="gpsimd")
            b.ts(g8[:, :], g8[:, :], gvc[:, 0:1], ALU.is_ge, rd=[g8, gvc],
                 wr=[g8])
            b.dma(m8_o[rs, :], g8[:, :], rd=[g8], q="gpsimd")
    return b.finish()


def build_route():
    b = B("route")
    S = 32768
    m8 = b.inp("m8", (S, 8))
    gv = b.inp("gv", (S, 1))
    w1e = b.inp("w1e", (1024, 4096))
    w2e = b.inp("w2e", (4096, 1024))
    b1e = b.inp("b1e", (128, 32))
    b2e = b.inp("b2e", (1, 1024))
    I128 = b.inp("I128", (128, 128))
    U128 = b.inp("U128", (128, 128))
    ones1 = b.inp("ones1", (1, 128))
    onesc = b.inp("onesc", (128, 1))
    kg_o = b.outp("kg_o", (S, 8))
    We_o = b.outp("We_o", (1024, 1024), BF16)
    ce_o = b.outp("ce_o", (1, 1024))

    ident = b.sb(128, 128)
    U_s = b.sb(128, 128)
    ones1_s = b.sb(1, 128)
    onesc_s = b.sb(128, 1)
    b.dma(ident[:, :], I128, wr=[ident])
    b.dma(U_s[:, :], U128, wr=[U_s])
    b.dma(ones1_s[:, :], ones1, wr=[ones1_s])
    b.dma(onesc_s[:, :], onesc, wr=[onesc_s])

    mt = b.sb(128, 8)
    gt = b.sb(128, 1)
    cum = b.sb(128, 8)
    off = b.sb(1, 8)
    loc = b.sb(128, 2)
    kg = b.sb(128, 8)
    pA = b.ps(128, 512)
    pB = b.ps(128, 512)
    pC = b.ps(128, 128)
    b.memset(off[0:1, :], 0.0, wr=[off])
    for tt in range(256):
        rs = slice(tt * 128, (tt + 1) * 128)
        b.dma(mt[:, :], m8[rs, :], rd=[mt], wr=[mt])
        b.dma(gt[:, :], gv[rs, :], rd=[gt], wr=[gt], q="gpsimd")
        b.mm(pC[:, 0:8], U_s[:, :], mt[:, :], True, True, rd=[U_s, mt], wr=[pC])
        b.mm(pA[:, 0:8], ones1_s[0:1, :], off[0:1, :], True, True,
             rd=[ones1_s, off], wr=[pA])
        b.cp(cum[:, :], pC[:, 0:8], rd=[pC], wr=[cum])
        b.tt(ALU.add, cum[:, :], cum[:, :], pA[:, 0:8], rd=[cum, pA], wr=[cum])
        b.mm(pB[0:1, 0:8], onesc_s[:, 0:1], mt[:, :], True, True,
             rd=[onesc_s, mt], wr=[pB])
        b.tt(ALU.add, off[0:1, :], off[0:1, :], pB[0:1, 0:8], rd=[off, pB],
             wr=[off])
        b.tt(ALU.mult, kg[:, :], cum[:, :], mt[:, :], rd=[cum, mt], wr=[kg])
        b.red(loc[:, 0:1], kg[:, :], rd=[kg], wr=[loc])
        b.ts(loc[:, 1:2], loc[:, 0:1], 4096.5, ALU.is_lt, rd=[loc], wr=[loc])
        b.ts(kg[:, :], mt[:, :], loc[:, 1:2], ALU.mult, s2=gt[:, 0:1],
             op1=ALU.mult, rd=[mt, loc, gt], wr=[kg])
        b.dma(kg_o[rs, :], kg[:, :], rd=[kg], q="gpsimd")

    wtmp = b.sb(128, 4096)
    w1T = b.sb(128, 8 * 1024, BF16)
    w1Tb = b.sb(128, 32 * 128, BF16)
    w2b = b.sb(128, 32 * 1024, BF16)
    w2f = b.sb(128, 1024)
    for ft in range(32):
        b.dma(w2f[:, :], w2e[ft * 128:(ft + 1) * 128, :], rd=[w2f], wr=[w2f])
        b.cp(w2b[:, ft * 1024:(ft + 1) * 1024], w2f[:, :], rd=[w2f], wr=[w2b])
    Wes = b.sb(128, 512, BF16)
    for dt in range(8):
        b.dma(wtmp[:, :], w1e[dt * 128:(dt + 1) * 128, :], rd=[wtmp], wr=[wtmp])
        for ft in range(32):
            b.transp(pC[:, :], wtmp[:, ft * 128:(ft + 1) * 128], ident[:, :],
                     rd=[wtmp, ident], wr=[pC])
            b.cp(w1Tb[:, ft * 128:(ft + 1) * 128], pC[:, :], rd=[pC], wr=[w1Tb])
        for half in range(2):
            for ft in range(32):
                b.mm(pA[:, :], w1Tb[:, ft * 128:(ft + 1) * 128],
                     w2b[:, ft * 1024 + half * 512: ft * 1024 + (half + 1) * 512],
                     ft == 0, ft == 31, rd=[w1Tb, w2b], wr=[pA])
            b.cp(Wes[:, :], pA[:, :], rd=[pA], wr=[Wes])
            b.dma(We_o[dt * 128:(dt + 1) * 128, half * 512:(half + 1) * 512],
                  Wes[:, :], rd=[Wes], q="gpsimd")
    b1s = b.sb(128, 32)
    b1sb = b.sb(128, 32, BF16)
    b.dma(b1s[:, :], b1e, wr=[b1s])
    b.cp(b1sb[:, :], b1s[:, :], rd=[b1s], wr=[b1sb])
    ce = b.sb(1, 1024)
    b2s = b.sb(1, 1024)
    b.dma(b2s[0:1, :], b2e, wr=[b2s])
    for half in range(2):
        for ft in range(32):
            b.mm(pB[0:1, :], b1sb[:, ft:ft + 1],
                 w2b[:, ft * 1024 + half * 512: ft * 1024 + (half + 1) * 512],
                 ft == 0, ft == 31, rd=[b1sb, w2b], wr=[pB])
        b.tt(ALU.add, ce[0:1, half * 512:(half + 1) * 512], pB[0:1, 0:512],
             b2s[0:1, half * 512:(half + 1) * 512], rd=[pB, b2s], wr=[ce])
    b.dma(ce_o[0:1, :], ce[0:1, :], rd=[ce], q="gpsimd")
    return b.finish()


def build_fin():
    b = B("fin")
    TL = 4096
    y1 = b.inp("y1", (TL, 1024))
    x1 = b.inp("x1", (TL, 1024))
    s_in = b.inp("s_in", (TL, 1024))
    kg = b.inp("kg", (TL, 8))
    We = b.inp("We", (8192, 1024), BF16)
    ce = b.inp("ce", (8, 1024))
    l2g = b.inp("l2g", (1, 1024))
    l2b = b.inp("l2b", (1, 1024))
    I128 = b.inp("I128", (128, 128))
    ones1 = b.inp("ones1", (1, 128))
    out_o = b.outp("out_o", (TL, 1024))

    ident = b.sb(128, 128)
    ones1_s = b.sb(1, 128)
    b.dma(ident[:, :], I128, wr=[ident])
    b.dma(ones1_s[:, :], ones1, wr=[ones1_s])
    pA = b.ps(128, 512)
    pB = b.ps(128, 512)
    pC = b.ps(128, 128)

    Web = b.sb(128, 64 * 1024, BF16)
    for blk in range(64):
        b.dma(Web[:, blk * 1024:(blk + 1) * 1024],
              We[blk * 128:(blk + 1) * 128, :], rd=[Web], wr=[Web],
              q="sync" if blk % 2 == 0 else "gpsimd")
    ces = b.sb(8, 1024)
    b.dma(ces[:, :], ce, wr=[ces])
    l2g_b = b.sb(128, 1024)
    l2b_b = b.sb(128, 1024)
    rowf = b.sb(1, 1024)
    for dst, src in ((l2g_b, l2g), (l2b_b, l2b)):
        b.dma(rowf[0:1, :], src, rd=[rowf], wr=[rowf])
        for c0 in range(0, 1024, 512):
            b.mm(pA[:, :], ones1_s[0:1, :], rowf[0:1, c0:c0 + 512], True, True,
                 rd=[ones1_s, rowf], wr=[pA])
            b.cp(dst[:, c0:c0 + 512], pA[:, :], rd=[pA], wr=[dst])

    y1t = b.sb(128, 1024)
    y1T = b.sb(128, 1024, BF16)
    kgt = b.sb(128, 8)
    kgT = b.sb(8, 128)
    accm = b.sb(128, 1024)
    sq = b.sb(128, 1024)
    mc = b.sb(128, 8)
    xt = b.sb(128, 1024)
    st = b.sb(128, 1024)
    for tt in range(32):
        rs = slice(tt * 128, (tt + 1) * 128)
        b.dma(y1t[:, :], y1[rs, :], rd=[y1t], wr=[y1t])
        b.dma(kgt[:, :], kg[rs, :], rd=[kgt], wr=[kgt], q="gpsimd")
        for j in range(8):
            b.transp(pC[:, :], y1t[:, j * 128:(j + 1) * 128], ident[:, :],
                     rd=[y1t, ident], wr=[pC])
            b.cp(y1T[:, j * 128:(j + 1) * 128], pC[:, :], rd=[pC], wr=[y1T])
        b.transp(pC[0:8, :], kgt[:, 0:8], ident[:, :], rd=[kgt, ident], wr=[pC])
        b.cp(kgT[0:8, :], pC[0:8, :], rd=[pC], wr=[kgT])
        for half in range(2):
            b.mm(pA[:, :], kgT[0:8, :], ces[0:8, half * 512:(half + 1) * 512],
                 True, True, rd=[kgT, ces], wr=[pA])
            b.cp(accm[:, half * 512:(half + 1) * 512], pA[:, :], rd=[pA],
                 wr=[accm])
        for e in range(8):
            for half in range(2):
                pp = pA if half == 0 else pB
                for j in range(8):
                    blk = e * 8 + j
                    b.mm(pp[:, :], y1T[:, j * 128:(j + 1) * 128],
                         Web[:, blk * 1024 + half * 512: blk * 1024 + (half + 1) * 512],
                         j == 0, j == 7, rd=[y1T, Web], wr=[pp])
                hs = slice(half * 512, (half + 1) * 512)
                b.stt(accm[:, hs], pp[:, :], kgt[:, e:e + 1], accm[:, hs],
                      ALU.mult, ALU.add, rd=[pp, kgt, accm], wr=[accm])
        b.tt(ALU.add, accm[:, :], accm[:, :], y1t[:, :], rd=[accm, y1t],
             wr=[accm])
        _ln_tile(b, accm, accm, sq, mc, 1024, 1e-5)
        b.tt(ALU.mult, accm[:, :], accm[:, :], l2g_b[:, :], rd=[accm, l2g_b],
             wr=[accm])
        b.tt(ALU.add, accm[:, :], accm[:, :], l2b_b[:, :], rd=[accm, l2b_b],
             wr=[accm])
        b.act(accm[:, :], accm[:, :], AF.Silu, rd=[accm], wr=[accm])
        b.dma(xt[:, :], x1[rs, :], rd=[xt], wr=[xt])
        b.dma(st[:, :], s_in[rs, :], rd=[st], wr=[st], q="gpsimd")
        b.tt(ALU.mult, accm[:, :], accm[:, :], st[:, :], rd=[accm, st],
             wr=[accm])
        b.tt(ALU.add, accm[:, :], accm[:, :], xt[:, :], rd=[accm, xt],
             wr=[accm])
        b.dma(out_o[rs, :], accm[:, :], rd=[accm], q="gpsimd")
    return b.finish()


def _f(x):
    return np.ascontiguousarray(np.asarray(x, dtype=np.float32))


def _consts():
    I128 = np.eye(128, dtype=np.float32)
    tri_incl = (np.arange(128)[:, None] <= np.arange(128)[None, :]).astype(np.int8)
    tri_strict = (np.arange(128)[:, None] < np.arange(128)[None, :]).astype(np.int8)
    ones1 = np.ones((1, 128), dtype=np.float32)
    onesT = np.ones((128, 128), dtype=np.float32)
    ones2 = np.zeros((128, 2), dtype=np.float32)
    ones2[0:64, 0] = 1.0
    ones2[64:128, 1] = 1.0
    rmask = np.ones((128, 512), dtype=np.float32)
    rmask[:, [0, 128, 256, 384]] = 0.0
    return I128, tri_incl, tri_strict, ones1, onesT, ones2, rmask


def kernel(**inp):
    I128, tri_incl, tri_strict, ones1, onesT, ones2, rmask = _consts()
    cores = list(range(NCORES))

    x = _f(inp["L0_attn_x"])
    xf = x.reshape(32768, 1024)
    x4096 = np.ascontiguousarray(xf[:4096])

    nc = build_l0main()
    maps = []
    for i in cores:
        maps.append({
            "x": x4096,
            "Wq": _f(inp["L0_attn_Wq"][:, 128 * i:128 * (i + 1)]),
            "Wk": _f(inp["L0_attn_Wk"][:, 128 * i:128 * (i + 1)]),
            "Wv": _f(inp["L0_attn_Wv"][:, 128 * i:128 * (i + 1)]),
            "Wg": _f(inp["L0_attn_Wg"][:, 128 * i:128 * (i + 1)]),
            "w1": _f(inp["L0_ffn_w1"]),
            "w2": _f(inp["L0_ffn_w2"]),
            "ew": _f(inp["L0_ffn_expert_weights"][512 * i:512 * (i + 1)]),
            "xf": np.ascontiguousarray(x4096[512 * i:512 * (i + 1)]),
            "I128": I128, "maskI": tri_incl, "ones1": ones1, "onesT": onesT,
        })
    res = run_bass_kernel_spmd(nc, maps, core_ids=cores)
    o_full = np.concatenate([res.results[i]["o_h"] for i in cores], axis=1)
    g_full = np.concatenate([res.results[i]["gax"] for i in cores], axis=0)

    nc = build_l0fin()
    maps = []
    for i in cores:
        sl = slice(512 * i, 512 * (i + 1))
        maps.append({
            "o": np.ascontiguousarray(o_full[sl]),
            "g": np.ascontiguousarray(g_full[sl]),
            "xs": np.ascontiguousarray(x4096[sl]),
            "Wo": _f(inp["L0_attn_Wo"]),
            "I128": I128,
        })
    res = run_bass_kernel_spmd(nc, maps, core_ids=cores)
    x1 = np.concatenate([res.results[i]["x1o"] for i in cores] + [xf[4096:]],
                        axis=0)

    nc = build_pre()
    maa = np.stack([_f(inp["L1_attn_time_maa_x"]), _f(inp["L1_attn_time_maa_w"]),
                    _f(inp["L1_attn_time_maa_k"]), _f(inp["L1_attn_time_maa_v"]),
                    _f(inp["L1_attn_time_maa_r"]), _f(inp["L1_attn_time_maa_g"]),
                    _f(inp["L1_attn_time_decay"])], axis=1)
    mw2 = _f(inp["L1_attn_time_maa_w2"]).reshape(160, 1024)
    maps = []
    for i in cores:
        sl = slice(4096 * i, 4096 * (i + 1))
        prow = (np.zeros((1, 1024), np.float32) if (4096 * i) % 8192 == 0
                else np.ascontiguousarray(x1[4096 * i - 1:4096 * i]))
        maps.append({
            "x": np.ascontiguousarray(x1[sl]), "prow": prow,
            "Wr": _f(inp["L1_attn_Wr"]), "Wk": _f(inp["L1_attn_Wk"]),
            "Wv": _f(inp["L1_attn_Wv"]), "Wg": _f(inp["L1_attn_Wg"]),
            "maa": maa, "mw1": _f(inp["L1_attn_time_maa_w1"]), "mw2": mw2,
            "dw1": _f(inp["L1_attn_time_decay_w1"]),
            "dw2": _f(inp["L1_attn_time_decay_w2"]),
            "I128": I128,
        })
    res = run_bass_kernel_spmd(nc, maps, core_ids=cores)
    rT = np.concatenate([res.results[i]["rT_o"] for i in cores], axis=1)
    kT = np.concatenate([res.results[i]["kT_o"] for i in cores], axis=1)
    wT = np.concatenate([res.results[i]["wT_o"] for i in cores], axis=1)
    krow = np.concatenate([res.results[i]["krow_o"] for i in cores], axis=0)
    vrow = np.concatenate([res.results[i]["vrow_o"] for i in cores], axis=0)
    g_row = np.concatenate([res.results[i]["g_o"] for i in cores], axis=0)

    nc = build_wkv()
    u_flat = _f(inp["L1_attn_time_faaaa"]).reshape(1024, 1)
    maps = []
    for i in cores:
        rs = slice(128 * i, 128 * (i + 1))
        maps.append({
            "rT": np.ascontiguousarray(rT[rs]),
            "kT": np.ascontiguousarray(kT[rs]),
            "wT": np.ascontiguousarray(wT[rs]),
            "krow": np.ascontiguousarray(krow[:, rs]),
            "vrow": np.ascontiguousarray(vrow[:, rs]),
            "ucol": np.ascontiguousarray(u_flat[rs]),
            "I128": I128, "maskS": tri_strict, "ones2": ones2,
            "rmask": rmask, "onesF": onesT,
            "shiftM": np.eye(128, dtype=np.float32)[:, 64:128].copy(),
        })
    res = run_bass_kernel_spmd(nc, maps, core_ids=cores)
    y_full = np.concatenate([res.results[i]["y_o"] for i in cores], axis=1)

    nc = build_post()
    maps = []
    rowsets = []
    for i in cores:
        rows = np.concatenate([np.arange(l * 8192 + 1024 * i,
                                         l * 8192 + 1024 * (i + 1))
                               for l in range(4)])
        rowsets.append(rows)
        maps.append({
            "x": np.ascontiguousarray(x1[rows]),
            "y": np.ascontiguousarray(y_full[rows]),
            "g": np.ascontiguousarray(g_row[rows]),
            "lnw": _f(inp["L1_attn_ln_w"]).reshape(1, 1024),
            "lnb": _f(inp["L1_attn_ln_b"]).reshape(1, 1024),
            "Wo": _f(inp["L1_attn_Wo"]),
            "ipw": _f(inp["L1_ffn_in_proj_w"]),
            "ipb": _f(inp["L1_ffn_in_proj_b"]).reshape(1, 3072),
            "opw": _f(inp["L1_ffn_out_proj_w"]),
            "opb": _f(inp["L1_ffn_out_proj_b"]).reshape(1, 1024),
            "l1g": _f(inp["L1_ffn_ln1_g"]).reshape(1, 1024),
            "l1b": _f(inp["L1_ffn_ln1_b"]).reshape(1, 1024),
            "wg": _f(inp["L1_ffn_wg"]),
            "I128": I128, "ones1": ones1,
        })
    res = run_bass_kernel_spmd(nc, maps, core_ids=cores)
    s_full = np.zeros((32768, 1024), np.float32)
    y1_full = np.zeros((32768, 1024), np.float32)
    m8_full = np.zeros((32768, 8), np.float32)
    gv_full = np.zeros((32768, 1), np.float32)
    for i in cores:
        s_full[rowsets[i]] = res.results[i]["s_o"]
        y1_full[rowsets[i]] = res.results[i]["y1_o"]
        m8_full[rowsets[i]] = res.results[i]["m8_o"]
        gv_full[rowsets[i]] = res.results[i]["gv_o"]

    nc = build_route()
    U128 = np.transpose((np.arange(128)[:, None] >= np.arange(128)[None, :])
                        .astype(np.float32))
    maps = []
    for i in cores:
        maps.append({
            "m8": m8_full, "gv": gv_full,
            "w1e": _f(inp["L1_ffn_w1"][i]),
            "w2e": _f(inp["L1_ffn_w2"][i]),
            "b1e": np.ascontiguousarray(_f(inp["L1_ffn_b1"][i]).reshape(32, 128).T),
            "b2e": _f(inp["L1_ffn_b2"][i]).reshape(1, 1024),
            "I128": I128, "U128": U128, "ones1": ones1,
            "onesc": np.ones((128, 1), np.float32),
        })
    res = run_bass_kernel_spmd(nc, maps, core_ids=cores)
    kgate = res.results[0]["kg_o"]
    We_all = np.concatenate([np.asarray(res.results[i]["We_o"]) for i in cores],
                            axis=0)
    ce_all = np.concatenate([res.results[i]["ce_o"] for i in cores], axis=0)

    nc = build_fin()
    maps = []
    for i in cores:
        sl = slice(4096 * i, 4096 * (i + 1))
        maps.append({
            "y1": np.ascontiguousarray(y1_full[sl]),
            "x1": np.ascontiguousarray(x1[sl]),
            "s_in": np.ascontiguousarray(s_full[sl]),
            "kg": np.ascontiguousarray(kgate[sl]),
            "We": We_all, "ce": ce_all,
            "l2g": _f(inp["L1_ffn_ln2_g"]).reshape(1, 1024),
            "l2b": _f(inp["L1_ffn_ln2_b"]).reshape(1, 1024),
            "I128": I128, "ones1": ones1,
        })
    res = run_bass_kernel_spmd(nc, maps, core_ids=cores)
    out = np.concatenate([res.results[i]["out_o"] for i in cores], axis=0)
    return out.reshape(4, 8192, 1024)
